# version 06
# baseline (speedup 1.0000x reference)
import contextlib
import numpy as np

import concourse.bass as bass
import concourse.mybir as mybir
from concourse.bass_utils import run_bass_kernel_spmd

F32 = mybir.dt.float32
BF16 = mybir.dt.bfloat16
AF = mybir.ActivationFunctionType
ALU = mybir.AluOpType
AX = mybir.AxisListType

MAGIC = float(np.float32(3 * 2.0 ** 22))
T, C, H, HD = 2048, 1024, 16, 64
TT, CC = 16, 8


class Sem:
    def __init__(self, h):
        self.h, self.n = h, 0

    def inc(self, instr, k=1):
        instr.then_inc(self.h, k)
        self.n += k
        return self.n


def build(stop_after=None, dbg=False, ln1_mode="full", no_cc=False):
    nc = bass.Bass("TRN2", target_bir_lowering=False)
    x_in = nc.dram_tensor("x", [T, C], F32, kind="ExternalInput").ap()
    out = nc.dram_tensor("out", [T, C], F32, kind="ExternalOutput").ap()
    if dbg:
        dbg_qf = nc.dram_tensor("dbg_qf", [128, T], F32, kind="ExternalOutput").ap()
        dbg_kf = nc.dram_tensor("dbg_kf", [128, T], F32, kind="ExternalOutput").ap()
        dbg_vf = nc.dram_tensor("dbg_vf", [128, T], F32, kind="ExternalOutput").ap()
        dbg_pp = nc.dram_tensor("dbg_pp", [128, T], F32, kind="ExternalOutput").ap()
        dbg_v1 = nc.dram_tensor("dbg_v1", [128, 2080], F32, kind="ExternalOutput").ap()
        dbg_yp = nc.dram_tensor("dbg_yp", [128, 8 * T], F32, kind="ExternalOutput").ap()
        dbg_st = nc.dram_tensor("dbg_st", [128, 96], F32, kind="ExternalOutput").ap()
        dbg_ng = nc.dram_tensor("dbg_ng", [128, 2], F32, kind="ExternalOutput").ap()
        dbg_xf = nc.dram_tensor("dbg_xf", [128, CC * T], F32, kind="ExternalOutput").ap()
        dbg_xq = nc.dram_tensor("dbg_xq", [128, CC * T], BF16, kind="ExternalOutput").ap()
        dbg_x2 = nc.dram_tensor("dbg_x2", [128, TT * C], F32, kind="ExternalOutput").ap()
        dbg_x3 = nc.dram_tensor("dbg_x3", [128, TT * C], F32, kind="ExternalOutput").ap()
        dbg_qt = nc.dram_tensor("dbg_qt", [128, 3072], BF16, kind="ExternalOutput").ap()
    wq_in = nc.dram_tensor("wq", [H, HD, C], F32, kind="ExternalInput").ap()
    wk_in = nc.dram_tensor("wk", [H, HD, C], F32, kind="ExternalInput").ap()
    wv_in = nc.dram_tensor("wv", [H, HD, C], F32, kind="ExternalInput").ap()
    wo_in = nc.dram_tensor("wo", [C, C], F32, kind="ExternalInput").ap()
    g0_in = nc.dram_tensor("g0", [4096, C], F32, kind="ExternalInput").ap()
    u0_in = nc.dram_tensor("u0", [4096, C], F32, kind="ExternalInput").ap()
    d0_in = nc.dram_tensor("d0", [C, 4096], F32, kind="ExternalInput").ap()
    wa_in = nc.dram_tensor("wa", [C, 1152], F32, kind="ExternalInput").ap()
    wp_in = nc.dram_tensor("wp", [C, C], F32, kind="ExternalInput").ap()
    g1_in = nc.dram_tensor("g1", [3072, C], F32, kind="ExternalInput").ap()
    u1_in = nc.dram_tensor("u1", [3072, C], F32, kind="ExternalInput").ap()
    d1_in = nc.dram_tensor("d1", [C, 3072], F32, kind="ExternalInput").ap()

    scrC = nc.dram_tensor("scrC", [1, 8], F32).ap()
    if dbg:
        scr_h = nc.dram_tensor("scr_h", [T, 4096], F32, kind="ExternalOutput").ap()
        dbg_bc = nc.dram_tensor("dbg_bc", [128, 8], F32, kind="ExternalOutput").ap()
        scr_wd0 = nc.dram_tensor("scr_wd0", [4096, C], BF16, kind="ExternalOutput").ap()
    else:
        scr_h = nc.dram_tensor("scr_h", [T, 4096], F32).ap()
        scr_wd0 = nc.dram_tensor("scr_wd0", [4096, C], BF16).ap()
    kw = {"kind": "ExternalOutput"} if dbg else {}
    scr_g = nc.dram_tensor("scr_g", [T, 3072], F32, **kw).ap()
    scr_u = nc.dram_tensor("scr_u", [T, 3072], F32, **kw).ap()
    scr_g2 = nc.dram_tensor("scr_g2", [T, 3072], F32, **kw).ap()
    scr_wgu = [nc.dram_tensor("scr_wgu0", [C, 8192], BF16).ap(),
               nc.dram_tensor("scr_wgu1", [C, 6144], BF16, **kw).ap()]
    scr_wd = [scr_wd0,
              nc.dram_tensor("scr_wd1", [3072, C], BF16).ap()]
    cc_in = nc.dram_tensor("cc_in", [1, 2], F32).ap()
    cc_out = nc.dram_tensor("cc_out", [1, 2], F32).ap()

    es = contextlib.ExitStack()
    with es:
        sbt = lambda *a: es.enter_context(nc.sbuf_tensor(*a))
        psa = lambda *a: es.enter_context(nc.psum_tensor(*a))
        mks = lambda n: Sem(es.enter_context(nc.semaphore(n)))

        xm = sbt("xm", [128, TT, C], F32)
        xfm = sbt("xfm", [128, CC, T], BF16)
        xbtm = sbt("xbtm", [128, C], BF16)
        st = sbt("st", [128, 96], F32)
        ident = sbt("ident", [128, 128], BF16)
        identf = sbt("identf", [128, 128], F32)
        mask128 = sbt("mask128", [128, 128], BF16)
        onesb = sbt("onesb", [1, 128], BF16)
        den = sbt("den", [1, 512], F32)
        denb = sbt("denb", [1, 512], BF16)
        negC = sbt("negC", [128, 2], F32)
        bcol = sbt("bcol", [128, 8], F32)
        kz = sbt("kz", [128, 4], BF16)
        sc1 = sbt("sc1", [1, 8], F32)
        big1 = sbt("big1", [128, 16384], BF16)
        big2 = sbt("big2", [128, 20512], BF16)
        hblk = sbt("hblk", [128, 4096], F32)

        ps_q = psa("ps_q", [128, 512], F32)
        ps_sc = psa("ps_sc", [128, 1024], F32)
        ps_av = [psa(f"ps_av{i}", [128, 512], F32) for i in range(4)]
        pt = psa("pt", [128, 1024], BF16)

        dW = mks("dW")
        dG = mks("dG")
        sT = mks("sT")
        sV = mks("sV")
        sS = mks("sS")
        sG = mks("sG")
        dD2 = [mks("dD0"), mks("dD1")]
        dS2 = [mks("dS0"), mks("dS1")]
        dPS = mks("dPS")

        block = es.enter_context(nc.Block())

        ypair = big1[:].rearrange("p (a t) -> p a t", a=8)
        hqfm = big1[:].rearrange("p (a t) -> p a t", a=32)
        wjs = big2[:, 0:1024]
        wtj = big2[:, 1024:2048].rearrange("p (a b) -> p a b", a=8)
        qf = big2[:, 2048:4096]
        kf = big2[:, 4096:6144]
        vf = big2[:, 6144:8192]
        pp = big2[:, 8192:10240]
        v1 = big2[:, 10240:12320].rearrange("p (t o) -> p t o", o=130)
        wos = big2[:, 12320:20512].rearrange("p (a b) -> p a b", a=8)
        qtile = big2[:, 0:8192]
        wgu = big2[:, 8192:16384].rearrange("p (a b c) -> p a b c", a=2, b=8)
        wdd = big2[:, 16384:18432].rearrange("p (a b) -> p a b", a=2)

        def B(name):
            return getattr(block, name)

        pt_free = [0]
        wjs_consumed = [0]
        wtj_consumed = [0]
        qkv_evac = [0]
        oproj_evac = [0]
        bred = [0]
        negc_tok = [0]
        last_exp = [0]
        head_prev_av = [0]
        bcast_evac = [0]
        v1ones_tok = [0]
        lnre_evac = [0]
        kdup_tok = [0]
        ffn = {}

        @B("sync")
        def _(sync):
            xt = x_in.rearrange("(n p) c -> p n c", p=128)
            for i in range(4):
                dW.inc(sync.dma_start(out=xm[:, i * 4:(i + 1) * 4, :],
                                      in_=xt[:, i * 4:(i + 1) * 4, :]), 16)
        tok_xload = dW.n

        @B("gpsimd")
        def _(g):
            g.memset(ident[:], 0.0)
            g.affine_select(out=ident[:], in_=ident[:], compare_op=ALU.not_equal,
                            fill=1.0, base=0, pattern=[[-1, 128]], channel_multiplier=1)
            g.memset(identf[:], 0.0)
            g.affine_select(out=identf[:], in_=identf[:], compare_op=ALU.not_equal,
                            fill=1.0, base=0, pattern=[[-1, 128]], channel_multiplier=1)
            g.memset(mask128[:], 1.0)
            g.affine_select(out=mask128[:], in_=mask128[:], compare_op=ALU.is_ge,
                            fill=0.0, base=0, pattern=[[1, 128]], channel_multiplier=-1)
            g.memset(onesb[:], 1.0)
            sG.inc(g.nop())
        tok_consts = sG.n

        def tm_tile_to_xfm(tt, vtok):
            @B("tensor")
            def _(t, tt=tt, vtok=vtok):
                t.wait_ge(sV.h, vtok)
                t.wait_ge(sG.h, tok_consts)
                t.wait_ge(sV.h, pt_free[0])
                for cc in range(CC):
                    ins = t.transpose(pt[:, cc * 128:(cc + 1) * 128],
                                      xbtm[:, cc * 128:(cc + 1) * 128], ident[:])
                sT.inc(ins)
            t6 = sT.n

            @B("vector")
            def _(v, tt=tt, t6=t6):
                v.wait_ge(sT.h, t6)
                for cc in range(CC):
                    ins = v.tensor_copy(out=xfm[:, cc, tt * 128:(tt + 1) * 128],
                                        in_=pt[:, cc * 128:(cc + 1) * 128])
                sV.inc(ins)
            pt_free[0] = sV.n

        def ln_retranspose(first=False, mode="full"):
            for tt in range(TT):
                if not first:
                    @B("vector")
                    def _(v, tt=tt):
                        sV.inc(v.reduce_sum(out=st[:, 0:1], in_=xm[:, tt, :], axis=AX.X))
                    t0 = sV.n

                    @B("vector")
                    def _(v, tt=tt, t0=t0):
                        v.wait_ge(sV.h, t0)
                        sV.inc(v.tensor_scalar(out=st[:, 1:2], in0=st[:, 0:1],
                                               scalar1=1.0 / C, scalar2=None, op0=ALU.mult))
                    t1 = sV.n

                    @B("scalar")
                    def _(s, tt=tt, t1=t1):
                        s.wait_ge(sV.h, t1)
                        sS.inc(s.activation(out=hblk[:, 0:C], in_=xm[:, tt, :],
                                            func=AF.Square, accum_out=st[:, 2:3]))
                    t2 = sS.n

                    @B("vector")
                    def _(v, tt=tt, t2=t2):
                        v.wait_ge(sS.h, t2)
                        v.tensor_scalar(out=st[:, 3:4], in0=st[:, 2:3],
                                        scalar1=1.0 / C, scalar2=None, op0=ALU.mult)
                        v.tensor_tensor(out=st[:, 4:5], in0=st[:, 1:2], in1=st[:, 1:2],
                                        op=ALU.mult)
                        v.tensor_tensor(out=st[:, 3:4], in0=st[:, 3:4], in1=st[:, 4:5],
                                        op=ALU.subtract)
                        sV.inc(v.tensor_scalar(out=st[:, 3:4], in0=st[:, 3:4],
                                               scalar1=1e-5, scalar2=None, op0=ALU.add))
                    t3a = sV.n

                    @B("vector")
                    def _(v, tt=tt, t3a=t3a):
                        v.wait_ge(sV.h, t3a)
                        sV.inc(v.reciprocal(out=st[:, 4:5], in_=st[:, 3:4]))
                    t3 = sV.n

                    @B("scalar")
                    def _(s, tt=tt, t3=t3):
                        s.wait_ge(sV.h, t3)
                        sS.inc(s.activation(out=st[:, 5:6], in_=st[:, 4:5], func=AF.Sqrt))
                    t4 = sS.n

                    @B("vector")
                    def _(v, tt=tt, t4=t4, mode=mode):
                        v.wait_ge(sS.h, t4)
                        dstx = xm[:, tt, :] if mode != "stats" else hblk[:, 1024:1024 + C]
                        v.tensor_scalar(out=dstx, in0=xm[:, tt, :],
                                        scalar1=st[:, 1:2], scalar2=st[:, 5:6],
                                        op0=ALU.subtract, op1=ALU.mult)
                        if mode == "stats":
                            v.tensor_copy(out=hblk[:, 2048 + tt * 8:2054 + tt * 8],
                                          in_=st[:, 0:6])
                        sV.inc(v.tensor_copy(out=xbtm[:, 0:C], in_=dstx))
                    t5 = sV.n
                else:
                    @B("vector")
                    def _(v, tt=tt):
                        if tt == 0:
                            v.wait_ge(dW.h, tok_xload)
                        sV.inc(v.tensor_copy(out=xbtm[:, 0:C], in_=xm[:, tt, :]))
                    t5 = sV.n
                if mode in ("norm", "stats"):
                    continue
                tm_tile_to_xfm(tt, t5)

        def scalar_to_bcol(src_ap, col, producer_sem, producer_tok, scr_off):
            @B("gpsimd")
            def _(g):
                g.wait_ge(producer_sem.h, producer_tok)
                dG.inc(g.dma_start(out=scrC[0:1, scr_off:scr_off + 1], in_=src_ap), 16)
                g.wait_ge(dG.h, dG.n)
                dG.inc(g.dma_start(out=bcol[:, col:col + 1],
                                   in_=scrC[0:1, scr_off:scr_off + 1].partition_broadcast(128)), 16)
                g.wait_ge(dG.h, dG.n)
                sG.inc(g.nop())
            return sG.n

        def emit_qkv_mm(layer, hp, j, wtok=None, dma_tok=None, kv=False):
            dst = (qf, kf, vf)[j] if not kv else kf
            scale = 0.125 if j == 0 else None
            src_w = wtj
            for qs in range(4):
                @B("tensor")
                def _(t, qs=qs, wtok=wtok, dma_tok=dma_tok, src_w=src_w):
                    if qs == 0 and wtok is not None:
                        t.wait_ge(sV.h, wtok)
                    if qs == 0 and dma_tok is not None:
                        t.wait_ge(dG.h, dma_tok)
                    t.wait_ge(sV.h, qkv_evac[0])
                    for cc in range(CC):
                        ins = t.matmul(ps_q[:, 0:512], src_w[:, cc, :],
                                       xfm[:, cc, qs * 512:(qs + 1) * 512],
                                       start=(cc == 0), stop=(cc == CC - 1))
                    sT.inc(ins)
                tk = sT.n
                if qs == 3:
                    wtj_consumed[0] = tk

                @B("vector")
                def _(v, qs=qs, tk=tk, scale=scale, dst=dst):
                    v.wait_ge(sT.h, tk)
                    if scale is not None:
                        ins = v.tensor_scalar(out=dst[:, qs * 512:(qs + 1) * 512],
                                              in0=ps_q[:, 0:512], scalar1=scale,
                                              scalar2=None, op0=ALU.mult)
                    else:
                        ins = v.tensor_copy(out=dst[:, qs * 512:(qs + 1) * 512],
                                            in_=ps_q[:, 0:512])
                    sV.inc(ins)
                qkv_evac[0] = sV.n

        def emit_attn_prepost(layer, hp):
            vsrc, vrow0 = (vf, 0) if layer == 0 else (kf, 64)
            if layer == 0 or hp == 0:
                for half in range(2):
                    @B("tensor")
                    def _(t, half=half):
                        t.wait_ge(sV.h, qkv_evac[0])
                        t.wait_ge(sV.h, pt_free[0])
                        for i in range(8):
                            tt = half * 8 + i
                            src = vsrc[vrow0:vrow0 + (128 if layer == 0 else 64),
                                       tt * 128:(tt + 1) * 128]
                            idn = ident[:] if layer == 0 else ident[64:128, 64:128]
                            ins = t.transpose(
                                pt[:, i * 128:i * 128 + (128 if layer == 0 else 64)],
                                src, idn)
                        sT.inc(ins)
                    tk = sT.n

                    @B("vector")
                    def _(v, half=half, tk=tk):
                        v.wait_ge(sT.h, tk)
                        v.wait_ge(sV.h, v1ones_tok[0])
                        for i in range(8):
                            tt = half * 8 + i
                            if layer == 0:
                                v.tensor_copy(out=v1[:, tt, 0:64],
                                              in_=pt[:, i * 128:i * 128 + 64])
                                ins = v.tensor_copy(out=v1[:, tt, 65:129],
                                                    in_=pt[:, i * 128 + 64:(i + 1) * 128])
                            else:
                                ins = v.tensor_copy(out=v1[:, tt, 0:64],
                                                    in_=pt[:, i * 128:i * 128 + 64])
                        sV.inc(ins)
                    pt_free[0] = sV.n
            if layer == 1 and hp == 0:
                @B("gpsimd")
                def _(g):
                    g.wait_ge(sV.h, pt_free[0])
                    dG.inc(g.dma_start(out=kf[64:128, 0:T], in_=kf[0:64, 0:T]), 16)
                    g.wait_ge(dG.h, dG.n)
                    sG.inc(g.nop())
                kdup_tok[0] = sG.n

            @B("vector")
            def _(v):
                if layer == 1 and hp == 0:
                    v.wait_ge(sG.h, kdup_tok[0])
                sV.inc(v.reduce_max(out=st[:, 10:11], in_=kf[:, 0:T], axis=AX.X,
                                    apply_absolute_value=True))
            tkd = sV.n

            @B("vector")
            def _(v, tkd=tkd):
                v.wait_ge(sV.h, tkd)
                v.memset(kz[:, 0:2], 0.0)
                v.tensor_copy(out=kz[0:64, 0:1], in_=st[0:64, 10:11])
                sV.inc(v.tensor_copy(out=kz[64:128, 1:2], in_=st[64:128, 10:11]))
            tok_kz = sV.n

            @B("scalar")
            def _(s):
                s.wait_ge(sV.h, pt_free[0] if layer == 0 else qkv_evac[0])
                if layer == 1:
                    s.wait_ge(sV.h, pt_free[0])
                sS.inc(s.activation(out=vf[:, 0:T], in_=qf[:, 0:T], func=AF.Abs))
            tok_absq = sS.n

            for bs in range(4):
                @B("tensor")
                def _(t, bs=bs):
                    t.wait_ge(sV.h, tok_kz)
                    t.wait_ge(sS.h, tok_absq)
                    t.wait_ge(sV.h, bred[0])
                    sT.inc(t.matmul(ps_q[0:2, 0:512], kz[:, 0:2],
                                    vf[:, bs * 512:(bs + 1) * 512], start=True, stop=True))
                tk = sT.n

                @B("vector")
                def _(v, bs=bs, tk=tk):
                    v.wait_ge(sT.h, tk)
                    sV.inc(v.reduce_max(out=st[0:2, 12 + bs:13 + bs],
                                        in_=ps_q[0:2, 0:512], axis=AX.X))
                bred[0] = sV.n

            @B("vector")
            def _(v):
                v.wait_ge(sV.h, bred[0])
                sV.inc(v.reduce_max(out=st[0:2, 11:12], in_=st[0:2, 12:16], axis=AX.X))
            tnm = sV.n

            @B("vector")
            def _(v, tnm=tnm):
                v.wait_ge(sV.h, tnm)
                sV.inc(v.tensor_scalar(out=st[0:2, 8:9], in0=st[0:2, 11:12],
                                       scalar1=-1.0, scalar2=None, op0=ALU.mult))
            tok_negc = sV.n

            @B("gpsimd")
            def _(g, tok_negc=tok_negc):
                g.wait_ge(sV.h, tok_negc)
                g.wait_ge(sS.h, last_exp[0])
                dG.inc(g.dma_start(out=scrC[0:1, 0:2].rearrange("o f -> f o"),
                                   in_=st[0:2, 8:9]), 16)
                g.wait_ge(dG.h, dG.n)
                dG.inc(g.dma_start(out=negC[:, 0:1],
                                   in_=scrC[0:1, 0:1].partition_broadcast(128)), 16)
                dG.inc(g.dma_start(out=negC[:, 1:2],
                                   in_=scrC[0:1, 1:2].partition_broadcast(128)), 16)
                g.wait_ge(dG.h, dG.n)
                sG.inc(g.nop())
            negc_tok[0] = sG.n

        def emit_head_attention(layer, hp, lh):
            row0 = 64 * lh
            vcol0 = 65 * lh if layer == 0 else 0
            head_evac_done = sV.n
            exp_toks, mask_toks, av_last = {}, {}, {}
            prev_exp_par = {}
            for kt in range(TT):
                qb0 = kt // 4
                for qb in range(qb0, 4):
                    par = qb % 2

                    @B("tensor")
                    def _(t, kt=kt, qb=qb, par=par):
                        if kt == 0 and qb == qb0:
                            t.wait_ge(sG.h, negc_tok[0])
                            t.wait_ge(sV.h, pt_free[0])
                            t.wait_ge(sV.h, bcast_evac[0])
                        if ('sc', par) in prev_exp_par:
                            t.wait_ge(sS.h, prev_exp_par[('sc', par)])
                        sT.inc(t.matmul(ps_sc[:, par * 512:(par + 1) * 512],
                                        kf[row0:row0 + 64, kt * 128:(kt + 1) * 128],
                                        qf[row0:row0 + 64, qb * 512:(qb + 1) * 512],
                                        start=True, stop=True))
                    tk_sc = sT.n

                    @B("scalar")
                    def _(s, kt=kt, qb=qb, par=par, tk_sc=tk_sc, lh=lh):
                        s.wait_ge(sT.h, tk_sc)
                        if (kt - 1, qb) in av_last:
                            s.wait_ge(sT.h, av_last[(kt - 1, qb)])
                        elif kt == 0:
                            s.wait_ge(sT.h, head_prev_av[0])
                        sS.inc(s.activation(out=pp[:, qb * 512:(qb + 1) * 512],
                                            in_=ps_sc[:, par * 512:(par + 1) * 512],
                                            func=AF.Exp, bias=negC[:, lh:lh + 1],
                                            scale=1.0))
                    exp_toks[(kt, qb)] = sS.n
                    prev_exp_par[('sc', par)] = sS.n
                    last_exp[0] = sS.n

                @B("vector")
                def _(v, kt=kt, qb0=qb0):
                    v.wait_ge(sS.h, exp_toks[(kt, qb0)])
                    if kt % 4 != 0:
                        v.memset(pp[:, qb0 * 512:kt * 128], 0.0)
                    sV.inc(v.tensor_tensor(out=pp[:, kt * 128:(kt + 1) * 128],
                                           in0=pp[:, kt * 128:(kt + 1) * 128],
                                           in1=mask128[:], op=ALU.mult))
                mask_toks[kt] = sV.n

                for qb in range(qb0, 4):
                    @B("tensor")
                    def _(t, kt=kt, qb=qb, qb0=qb0):
                        t.wait_ge(sS.h, exp_toks[(kt, qb)])
                        if qb == qb0:
                            t.wait_ge(sV.h, mask_toks[kt])
                        if kt == 0:
                            t.wait_ge(sV.h, head_evac_done)
                        sT.inc(t.matmul(ps_av[qb][0:65, 0:512],
                                        v1[:, kt, vcol0:vcol0 + 65],
                                        pp[:, qb * 512:(qb + 1) * 512],
                                        start=(kt == 0), stop=(kt == qb * 4 + 3)))
                    av_last[(kt, qb)] = sT.n
            head_prev_av[0] = sT.n

            for qb in range(4):
                @B("vector")
                def _(v, qb=qb):
                    v.wait_ge(sT.h, av_last[(qb * 4 + 3, qb)])
                    v.tensor_copy(out=den[0:1, 0:512], in_=ps_av[qb][64:65, 0:512])
                    v.reciprocal(out=den[0:1, 0:512], in_=den[0:1, 0:512])
                    sV.inc(v.tensor_copy(out=denb[0:1, 0:512], in_=den[0:1, 0:512]))
                tden = sV.n

                @B("tensor")
                def _(t, qb=qb, tden=tden):
                    t.wait_ge(sV.h, tden)
                    if ('sc', qb % 2) in prev_exp_par:
                        t.wait_ge(sS.h, prev_exp_par[('sc', qb % 2)])
                    t.wait_ge(sV.h, bcast_evac[0])
                    sT.inc(t.matmul(ps_sc[:, (qb % 2) * 512:(qb % 2 + 1) * 512],
                                    onesb[0:1, 0:128], denb[0:1, 0:512],
                                    start=True, stop=True))
                tbc = sT.n

                @B("vector")
                def _(v, qb=qb, tbc=tbc, hp=hp, row0=row0):
                    v.wait_ge(sT.h, tbc)
                    v.tensor_copy(out=ypair[row0:row0 + 64, hp, qb * 512:(qb + 1) * 512],
                                  in_=ps_av[qb][0:64, 0:512])
                    sV.inc(v.tensor_tensor(
                        out=ypair[row0:row0 + 64, hp, qb * 512:(qb + 1) * 512],
                        in0=ypair[row0:row0 + 64, hp, qb * 512:(qb + 1) * 512],
                        in1=ps_sc[row0:row0 + 64, (qb % 2) * 512:(qb % 2) * 512 + 512],
                        op=ALU.mult))
                bcast_evac[0] = sV.n

        def attention(layer):
            bar_T, bar_V = sT.n, sV.n
            @B("gpsimd")
            def _(g, bar_T=bar_T, bar_V=bar_V):
                w = wo_in if layer == 0 else wp_in
                g.wait_ge(sT.h, bar_T)
                g.wait_ge(sV.h, bar_V)
                for p8 in range(8):
                    dG.inc(g.dma_start(out=wos[:, p8, :],
                                       in_=w[p8 * 128:(p8 + 1) * 128, :]), 16)
            tok_wo = dG.n

            @B("vector")
            def _(v):
                v.wait_ge(sT.h, head_prev_av[0])
                v.memset(v1[:, :, 64:65].rearrange("p t o -> p (t o)"), 1.0)
                sV.inc(v.memset(v1[:, :, 129:130].rearrange("p t o -> p (t o)"), 1.0))
            v1ones_tok[0] = sV.n

            for hp in range(8):
                if layer == 0:
                    for j, w in enumerate((wq_in, wk_in, wv_in)):
                        @B("gpsimd")
                        def _(g, j=j, w=w, hp=hp):
                            g.wait_ge(sT.h, wjs_consumed[0])
                            src = w.rearrange("h d c -> (h d) c")[hp * 128:(hp + 1) * 128, :]
                            dG.inc(g.dma_start(out=wjs[:, :], in_=src), 16)
                        tw = dG.n

                        @B("tensor")
                        def _(t, tw=tw):
                            t.wait_ge(dG.h, tw)
                            t.wait_ge(sV.h, pt_free[0])
                            for cc in range(CC):
                                ins = t.transpose(pt[:, cc * 128:(cc + 1) * 128],
                                                  wjs[:, cc * 128:(cc + 1) * 128],
                                                  ident[:])
                            sT.inc(ins)
                        ttk = sT.n
                        wjs_consumed[0] = ttk

                        @B("vector")
                        def _(v, ttk=ttk):
                            v.wait_ge(sT.h, ttk)
                            v.wait_ge(sT.h, wtj_consumed[0])
                            sV.inc(v.tensor_copy(
                                out=wtj.rearrange("p a b -> p (a b)"),
                                in_=pt[:, 0:1024]))
                        pt_free[0] = sV.n
                        emit_qkv_mm(layer, hp, j, wtok=sV.n)
                else:
                    @B("gpsimd")
                    def _(g, hp=hp):
                        g.wait_ge(sT.h, wtj_consumed[0])
                        src = wa_in[:, hp * 128:(hp + 1) * 128].rearrange(
                            "(n p) c -> p n c", p=128)
                        dG.inc(g.dma_start(out=wtj, in_=src), 16)
                    tw = dG.n
                    emit_qkv_mm(layer, hp, 0, dma_tok=tw)
                    if hp == 0:
                        @B("gpsimd")
                        def _(g, dn=dG.n):
                            g.wait_ge(sT.h, wtj_consumed[0])
                            g.wait_ge(dG.h, dn)
                            src = wa_in[:, 1024:1152].rearrange("(n p) c -> p n c", p=128)
                            dG.inc(g.dma_start(out=wtj[:, :, 0:128], in_=src), 16)
                        tkv = dG.n
                        emit_qkv_mm(layer, hp, 1, dma_tok=tkv, kv=True)

                emit_attn_prepost(layer, hp)
                for lh in range(2):
                    emit_head_attention(layer, hp, lh)

            for tt in range(TT):
                for cb in range(2):
                    @B("tensor")
                    def _(t, tt=tt, cb=cb):
                        if tt == 0 and cb == 0:
                            t.wait_ge(dG.h, tok_wo)
                            t.wait_ge(sV.h, bcast_evac[0])
                        t.wait_ge(sV.h, oproj_evac[0])
                        for p8 in range(8):
                            ins = t.matmul(ps_q[:, 0:512],
                                           ypair[:, p8, tt * 128:(tt + 1) * 128],
                                           wos[:, p8, cb * 512:(cb + 1) * 512],
                                           start=(p8 == 0), stop=(p8 == 7))
                        sT.inc(ins)
                    tk = sT.n

                    @B("vector")
                    def _(v, tt=tt, cb=cb, tk=tk):
                        v.wait_ge(sT.h, tk)
                        sV.inc(v.tensor_tensor(out=xm[:, tt, cb * 512:(cb + 1) * 512],
                                               in0=ps_q[:, 0:512],
                                               in1=xm[:, tt, cb * 512:(cb + 1) * 512],
                                               op=ALU.add))
                    oproj_evac[0] = sV.n

        def prep_ternary(w_in, nrows, scr_dst, scr_is_wd, mean_col):
            ncols = w_in.shape[1]
            nch = nrows // 128
            sS_bar = sS.n
            @B("vector")
            def _(v):
                sV.inc(v.memset(st[:, 40:41], 0.0))
            for ch in range(nch):
                @B("sync")
                def _(sync, ch=ch, sS_bar=sS_bar):
                    if ch == 0:
                        sync.wait_ge(sS.h, sS_bar)
                    else:
                        sync.wait_ge(sV.h, ffn.get(('p1v', ch - 1), 0))
                    dW.inc(sync.dma_start(out=hblk[:, 0:ncols],
                                          in_=w_in[ch * 128:(ch + 1) * 128, :]), 16)
                ffn[('p1d', ch)] = dW.n

                @B("vector")
                def _(v, ch=ch):
                    v.wait_ge(dW.h, ffn[('p1d', ch)])
                    sV.inc(v.reduce_sum(out=st[:, 41:42], in_=hblk[:, 0:ncols], axis=AX.X,
                                        apply_absolute_value=True))
                tr1 = sV.n

                @B("vector")
                def _(v, tr1=tr1):
                    v.wait_ge(sV.h, tr1)
                    sV.inc(v.tensor_tensor(out=st[:, 40:41], in0=st[:, 40:41],
                                           in1=st[:, 41:42], op=ALU.add))
                ffn[('p1v', ch)] = sV.n
            @B("tensor")
            def _(t):
                t.wait_ge(sV.h, sV.n)
                t.wait_ge(sV.h, pt_free[0])
                sT.inc(t.transpose(ps_sc[0:1, 0:128], st[:, 40:41], identf[:]))
            tps = sT.n

            @B("vector")
            def _(v, tps=tps):
                v.wait_ge(sT.h, tps)
                sV.inc(v.reduce_sum(out=sc1[0:1, 0:1], in_=ps_sc[0:1, 0:128], axis=AX.X))
            trs = sV.n

            @B("vector")
            def _(v, trs=trs, sG_bar=sG.n):
                v.wait_ge(sV.h, trs)
                v.wait_ge(sG.h, sG_bar)
                sV.inc(v.tensor_scalar(out=sc1[0:1, 0:1], in0=sc1[0:1, 0:1],
                                       scalar1=1.0 / (nrows * ncols), scalar2=1e-5,
                                       op0=ALU.mult, op1=ALU.max))
            tmn0 = sV.n

            @B("vector")
            def _(v, tmn0=tmn0):
                v.wait_ge(sV.h, tmn0)
                sV.inc(v.reciprocal(out=sc1[0:1, 1:2], in_=sc1[0:1, 0:1]))
            tmean = sV.n
            g1 = scalar_to_bcol(sc1[0:1, 0:1], mean_col, sV, tmean, 4)
            g2 = scalar_to_bcol(sc1[0:1, 1:2], 7, sV, tmean, 5)

            for ch in range(nch):
                @B("sync")
                def _(sync, ch=ch, tmean=tmean):
                    if ch == 0:
                        sync.wait_ge(sV.h, tmean)
                    else:
                        sync.wait_ge(sV.h, ffn.get(('p2q', ch - 1), 0))
                    dW.inc(sync.dma_start(out=hblk[:, 0:ncols],
                                          in_=w_in[ch * 128:(ch + 1) * 128, :]), 16)
                ffn[('p2d', ch)] = dW.n

                @B("vector")
                def _(v, ch=ch, g2=g2):
                    v.wait_ge(dW.h, ffn[('p2d', ch)])
                    if ch == 0:
                        v.wait_ge(sG.h, g2)
                        v.wait_ge(sT.h, wtj_consumed[0])
                    v.tensor_scalar(out=hblk[:, 0:ncols], in0=hblk[:, 0:ncols],
                                    scalar1=bcol[:, 7:8], scalar2=MAGIC,
                                    op0=ALU.mult, op1=ALU.add)
                    v.tensor_scalar(out=hblk[:, 0:ncols], in0=hblk[:, 0:ncols],
                                    scalar1=MAGIC, scalar2=1.0,
                                    op0=ALU.subtract, op1=ALU.min)
                    sV.inc(v.tensor_scalar(out=qtile[:, 0:ncols], in0=hblk[:, 0:ncols],
                                           scalar1=-1.0, scalar2=None, op0=ALU.max))
                ffn[('p2q', ch)] = sV.n

                nblk = ncols // 128
                for b0 in range(0, nblk, 8):
                    nb = min(8, nblk - b0)
                    @B("tensor")
                    def _(t, ch=ch, b0=b0, nb=nb):
                        t.wait_ge(sV.h, ffn[('p2q', ch)])
                        t.wait_ge(sV.h, pt_free[0])
                        for i in range(nb):
                            ins = t.transpose(pt[:, i * 128:(i + 1) * 128],
                                              qtile[:, (b0 + i) * 128:(b0 + i + 1) * 128],
                                              ident[:])
                        sT.inc(ins)
                    tk = sT.n

                    @B("vector")
                    def _(v, ch=ch, b0=b0, nb=nb, tk=tk):
                        v.wait_ge(sT.h, tk)
                        v.wait_ge(dPS.h, ffn.get(('p2s', ch, b0 - 8), 0))
                        sV.inc(v.tensor_copy(out=xbtm[:, 0:nb * 128].bitcast(BF16),
                                             in_=pt[:, 0:nb * 128]))
                    pt_free[0] = sV.n
                    ffn[('p2t', ch, b0)] = sV.n

                    @B("sync")
                    def _(sync, ch=ch, b0=b0, nb=nb):
                        sync.wait_ge(sV.h, ffn[('p2t', ch, b0)])
                        if scr_is_wd:
                            dst = scr_dst.rearrange("(n p) c -> p n c", p=128)[
                                :, b0:b0 + nb, ch * 128:(ch + 1) * 128]
                        else:
                            dst = scr_dst.rearrange("(n p) f -> p n f", p=128)[
                                :, b0:b0 + nb, ch * 128:(ch + 1) * 128]
                        dPS.inc(sync.dma_start(
                            out=dst,
                            in_=xbtm[:, 0:nb * 128].rearrange("p (n c) -> p n c", c=128)), 16)
                    ffn[('p2s', ch, b0)] = dPS.n
                @B("vector")
                def _(v, ch=ch):
                    v.wait_ge(dPS.h, ffn[('p2s', ch, (ncols // 128 - 1) // 8 * 8)])
                    sV.inc(v.engine_nop())
                pt_free[0] = max(pt_free[0], sV.n)
            return sV.n

        def prep_lpbq(w_in, nrows, scr_dst, scr_is_wd):
            ncols = w_in.shape[1]
            nch = nrows // 128
            nb32 = ncols // 32
            sS_bar = sS.n
            for ch in range(nch):
                @B("sync")
                def _(sync, ch=ch, sS_bar=sS_bar):
                    if ch == 0:
                        sync.wait_ge(sS.h, sS_bar)
                    sync.wait_ge(sV.h, ffn.get(('lq', ch - 1), 0))
                    dW.inc(sync.dma_start(out=hblk[:, 0:ncols],
                                          in_=w_in[ch * 128:(ch + 1) * 128, :]), 16)
                ffn[('ld', ch)] = dW.n

                sc_s = hblk[:, 3840:3840 + nb32]
                sc_r = hblk[:, 3968:3968 + nb32]
                @B("vector")
                def _(v, ch=ch, sc_s=sc_s):
                    v.wait_ge(dW.h, ffn[('ld', ch)])
                    if ch == 0:
                        v.wait_ge(sT.h, wtj_consumed[0])
                    h3 = hblk[:, 0:ncols].rearrange("p (b o) -> p b o", o=32)
                    sV.inc(v.tensor_reduce(out=sc_s, in_=h3, axis=AX.X,
                                           op=ALU.max, apply_absolute_value=True))
                trl = sV.n

                @B("vector")
                def _(v, trl=trl, sc_s=sc_s):
                    v.wait_ge(sV.h, trl)
                    sV.inc(v.tensor_scalar(out=sc_s, in0=sc_s,
                                           scalar1=1.0 / 7.0, scalar2=1e-12,
                                           op0=ALU.mult, op1=ALU.max))
                trl1b = sV.n

                @B("vector")
                def _(v, trl1b=trl1b, sc_s=sc_s, sc_r=sc_r):
                    v.wait_ge(sV.h, trl1b)
                    sV.inc(v.reciprocal(out=sc_r, in_=sc_s))
                trl2 = sV.n

                @B("vector")
                def _(v, ch=ch, trl2=trl2, sc_s=sc_s, sc_r=sc_r):
                    v.wait_ge(sV.h, trl2)
                    h3 = hblk[:, 0:ncols].rearrange("p (b o) -> p b o", o=32)
                    v.tensor_tensor(out=h3, in0=h3,
                                    in1=sc_r.unsqueeze(2).broadcast_to([128, nb32, 32]),
                                    op=ALU.mult)
                    v.tensor_scalar(out=hblk[:, 0:ncols], in0=hblk[:, 0:ncols],
                                    scalar1=MAGIC, scalar2=MAGIC,
                                    op0=ALU.add, op1=ALU.subtract)
                    sV.inc(v.tensor_tensor(
                        out=qtile[:, 0:ncols].rearrange("p (b o) -> p b o", o=32),
                        in0=hblk[:, 0:ncols].rearrange("p (b o) -> p b o", o=32),
                        in1=sc_s.unsqueeze(2).broadcast_to([128, nb32, 32]),
                        op=ALU.mult))
                ffn[('lq', ch)] = sV.n

                nblk = ncols // 128
                for b0 in range(0, nblk, 8):
                    nb = min(8, nblk - b0)
                    @B("tensor")
                    def _(t, ch=ch, b0=b0, nb=nb):
                        t.wait_ge(sV.h, ffn[('lq', ch)])
                        t.wait_ge(sV.h, pt_free[0])
                        for i in range(nb):
                            ins = t.transpose(pt[:, i * 128:(i + 1) * 128],
                                              qtile[:, (b0 + i) * 128:(b0 + i + 1) * 128],
                                              ident[:])
                        sT.inc(ins)
                    tk = sT.n

                    @B("vector")
                    def _(v, ch=ch, b0=b0, nb=nb, tk=tk):
                        v.wait_ge(sT.h, tk)
                        v.wait_ge(dPS.h, ffn.get(('ls', ch, b0 - 8), 0))
                        sV.inc(v.tensor_copy(out=xbtm[:, 0:nb * 128].bitcast(BF16),
                                             in_=pt[:, 0:nb * 128]))
                    pt_free[0] = sV.n
                    ffn[('lt', ch, b0)] = sV.n

                    @B("sync")
                    def _(sync, ch=ch, b0=b0, nb=nb):
                        sync.wait_ge(sV.h, ffn[('lt', ch, b0)])
                        if scr_is_wd:
                            dst = scr_dst.rearrange("(n p) c -> p n c", p=128)[
                                :, b0:b0 + nb, ch * 128:(ch + 1) * 128]
                        else:
                            dst = scr_dst.rearrange("(n p) f -> p n f", p=128)[
                                :, b0:b0 + nb, ch * 128:(ch + 1) * 128]
                        dPS.inc(sync.dma_start(
                            out=dst,
                            in_=xbtm[:, 0:nb * 128].rearrange("p (n c) -> p n c", c=128)), 16)
                    ffn[('ls', ch, b0)] = dPS.n
                @B("vector")
                def _(v, ch=ch):
                    v.wait_ge(dPS.h, ffn[('ls', ch, (ncols // 128 - 1) // 8 * 8)])
                    sV.inc(v.engine_nop())
                pt_free[0] = max(pt_free[0], sV.n)
            return sV.n

        def partition_max(src_col, dst_col):
            @B("tensor")
            def _(t):
                t.wait_ge(sV.h, sV.n)
                sT.inc(t.transpose(ps_sc[0:1, 0:128], st[:, src_col:src_col + 1],
                                   identf[:]))
            tk = sT.n

            @B("vector")
            def _(v, tk=tk):
                v.wait_ge(sT.h, tk)
                sV.inc(v.reduce_max(out=sc1[0:1, dst_col:dst_col + 1],
                                    in_=ps_sc[0:1, 0:128], axis=AX.X))
            return sV.n

        def allreduce2(val0_ap, val1_ap, vtok):
            @B("gpsimd")
            def _(g):
                g.wait_ge(sV.h, vtok)
                dG.inc(g.dma_start(out=cc_in[0:1, 0:1], in_=val0_ap), 16)
                dG.inc(g.dma_start(out=cc_in[0:1, 1:2], in_=val1_ap), 16)
                g.wait_ge(dG.h, dG.n)
                if no_cc:
                    dG.inc(g.dma_start(out=cc_out, in_=cc_in), 16)
                    g.wait_ge(dG.h, dG.n)
                else:
                    g.collective_compute("AllReduce", ALU.max,
                                         replica_groups=[list(range(8))],
                                         ins=[cc_in], outs=[cc_out])
                sG.inc(g.nop())
                dG.inc(g.dma_start(out=sc1[0:1, 4:6], in_=cc_out), 16)
                g.wait_ge(dG.h, dG.n)
                sG.inc(g.nop())
            return sG.n

        def ffn_layer(layer):
            I = 4096 if layer == 0 else 3072
            NI = I // 512
            wgu_scr = scr_wgu[layer]
            wd_scr = scr_wd[layer]

            if layer == 0:
                prep_ternary(g0_in, 4096, wgu_scr.rearrange("c (a) -> c a")[:, 0:4096], False, 0)
                prep_ternary(u0_in, 4096, wgu_scr[:, 4096:8192], False, 1)
                prep_ternary(d0_in, 1024, wd_scr, True, 2)
            else:
                prep_lpbq(g1_in, 3072, wgu_scr[:, 0:3072], False)
                prep_lpbq(u1_in, 3072, wgu_scr[:, 3072:6144], False)
                prep_lpbq(d1_in, 1024, wd_scr, True)

            if layer == 0:
                for tt in range(TT):
                    @B("vector")
                    def _(v, tt=tt):
                        sV.inc(v.reduce_max(out=st[:, 16 + tt:17 + tt], in_=xm[:, tt, :],
                                            axis=AX.X, apply_absolute_value=True))
                    trx = sV.n

                    @B("vector")
                    def _(v, tt=tt, trx=trx):
                        v.wait_ge(sV.h, trx)
                        sV.inc(v.tensor_scalar(out=st[:, 16 + tt:17 + tt],
                                               in0=st[:, 16 + tt:17 + tt],
                                               scalar1=1e-5, scalar2=1.0 / 127.0,
                                               op0=ALU.max, op1=ALU.mult))
                    trx2 = sV.n

                    @B("vector")
                    def _(v, tt=tt, trx2=trx2):
                        v.wait_ge(sV.h, trx2)
                        sV.inc(v.reciprocal(out=st[:, 32 + tt:33 + tt],
                                            in_=st[:, 16 + tt:17 + tt]))
                    trx3 = sV.n

                    @B("vector")
                    def _(v, tt=tt, trx3=trx3):
                        v.wait_ge(sV.h, trx3)
                        v.tensor_scalar(out=hblk[:, 0:C], in0=xm[:, tt, :],
                                        scalar1=st[:, 32 + tt:33 + tt], scalar2=MAGIC,
                                        op0=ALU.mult, op1=ALU.add)
                        sV.inc(v.tensor_scalar(out=xbtm[:, 0:C], in0=hblk[:, 0:C],
                                               scalar1=MAGIC, scalar2=None,
                                               op0=ALU.subtract))
                    tm_tile_to_xfm(tt, sV.n)
                @B("vector")
                def _(v):
                    v.wait_ge(sG.h, sG.n)
                    v.tensor_scalar(out=st[:, 48:64], in0=st[:, 16:32],
                                    scalar1=bcol[:, 0:1], scalar2=None, op0=ALU.mult)
                    sV.inc(v.tensor_scalar(out=st[:, 64:80], in0=st[:, 16:32],
                                           scalar1=bcol[:, 1:2], scalar2=None,
                                           op0=ALU.mult))
                tok_agau = sV.n
            else:
                @B("vector")
                def _(v):
                    sV.inc(v.reduce_max(out=st[:, 16:17],
                                        in_=xm[:].rearrange("p t c -> p (t c)"),
                                        axis=AX.X, apply_absolute_value=True))
                vtk = sV.n
                partition_max(16, 0)
                artok = allreduce2(sc1[0:1, 0:1], sc1[0:1, 0:1], sV.n)
                @B("vector")
                def _(v, artok=artok):
                    v.wait_ge(sG.h, artok)
                    sV.inc(v.tensor_scalar(out=sc1[0:1, 0:1], in0=sc1[0:1, 4:5],
                                           scalar1=1.0 / 32767.0, scalar2=1e-12,
                                           op0=ALU.mult, op1=ALU.max))
                tsx0 = sV.n

                @B("vector")
                def _(v, tsx0=tsx0):
                    v.wait_ge(sV.h, tsx0)
                    sV.inc(v.reciprocal(out=sc1[0:1, 1:2], in_=sc1[0:1, 0:1]))
                tsx = sV.n
                g1t = scalar_to_bcol(sc1[0:1, 0:1], 2, sV, tsx, 6)
                g2t = scalar_to_bcol(sc1[0:1, 1:2], 3, sV, tsx, 7)
                for tt in range(TT):
                    @B("vector")
                    def _(v, tt=tt, g2t=g2t):
                        if tt == 0:
                            v.wait_ge(sG.h, g2t)
                        v.tensor_scalar(out=hblk[:, 0:C], in0=xm[:, tt, :],
                                        scalar1=bcol[:, 3:4], scalar2=MAGIC,
                                        op0=ALU.mult, op1=ALU.add)
                        sV.inc(v.tensor_scalar(out=xbtm[:, 0:C], in0=hblk[:, 0:C],
                                               scalar1=MAGIC, scalar2=None,
                                               op0=ALU.subtract))
                    tm_tile_to_xfm(tt, sV.n)
                tok_agau = sV.n
                if dbg:
                    @B("gpsimd")
                    def _(g, tok_agau=tok_agau):
                        g.wait_ge(sV.h, tok_agau)
                        dG.inc(g.dma_start(out=dbg_xq,
                                           in_=xfm.rearrange("p a t -> p (a t)")), 16)
                        g.wait_ge(dG.h, dG.n)
                        sG.inc(g.nop())

            @B("vector")
            def _(v):
                sV.inc(v.memset(st[:, 16:32] if layer == 0 else st[:, 16:18], 0.0))
            s1e_hist = [0, 0]
            s1s_hist = [0, 0]
            for ic in range(NI):
                @B("sync")
                def _(sync, ic=ic):
                    sync.wait_ge(sT.h, ffn.get(('s1t', ic - 1), 0))
                    for gu in range(2):
                        src = wgu_scr.rearrange("(n p) f -> p n f", p=128)[
                            :, :, gu * I + ic * 512: gu * I + (ic + 1) * 512]
                        dW.inc(sync.dma_start(out=wgu[:, gu, :, :], in_=src), 16)
                ffn[('s1d', ic)] = dW.n

                for tt in range(TT):
                    g = ic * TT + tt
                    P = g % 2
                    @B("tensor")
                    def _(t, ic=ic, tt=tt, g=g):
                        if tt == 0:
                            t.wait_ge(dW.h, ffn[('s1d', ic)])
                            if ic == 0:
                                t.wait_ge(sV.h, tok_agau)
                        t.wait_ge(sV.h, s1e_hist[g])
                        for gu in range(2):
                            for cc in range(CC):
                                ins = t.matmul(ps_av[2 * (g % 2) + gu][:, 0:512],
                                               xfm[:, cc, tt * 128:(tt + 1) * 128],
                                               wgu[:, gu, cc, :],
                                               start=(cc == 0), stop=(cc == CC - 1))
                        sT.inc(ins)
                    ffn[('s1t', ic)] = sT.n
                    tkmm = sT.n

                    pg = ps_av[2 * (g % 2) + 0]
                    pu = ps_av[2 * (g % 2) + 1]
                    if layer == 0:
                        sil = hblk[:, 2048 + 512 * P:2560 + 512 * P]
                        hst = hblk[:, 512 * P:512 * P + 512]
                        @B("scalar")
                        def _(s, ic=ic, tt=tt, tkmm=tkmm, pg=pg, sil=sil):
                            s.wait_ge(sT.h, tkmm)
                            sS.inc(s.activation(out=sil, in_=pg[:, 0:512],
                                                func=AF.Silu, scale=st[:, 48 + tt:49 + tt]))
                        tsil = sS.n

                        @B("vector")
                        def _(v, ic=ic, tt=tt, g=g, tsil=tsil, pu=pu, sil=sil, hst=hst):
                            v.wait_ge(sS.h, tsil)
                            v.wait_ge(dS2[g % 2].h, s1s_hist[g])
                            v.scalar_tensor_tensor(out=hst, in0=pu[:, 0:512],
                                                   scalar=st[:, 64 + tt:65 + tt],
                                                   in1=sil,
                                                   op0=ALU.mult, op1=ALU.mult)
                            sV.inc(v.reduce_max(out=st[:, 41:42], in_=hst,
                                                axis=AX.X, apply_absolute_value=True))
                        trh = sV.n

                        @B("vector")
                        def _(v, tt=tt, trh=trh):
                            v.wait_ge(sV.h, trh)
                            sV.inc(v.tensor_tensor(out=st[:, 16 + tt:17 + tt],
                                                   in0=st[:, 16 + tt:17 + tt],
                                                   in1=st[:, 41:42], op=ALU.max))
                        ffn[('s1e', ic, tt)] = sV.n
                        s1e_hist.append(sV.n)

                        @B("sync")
                        def _(sync, ic=ic, tt=tt, g=g, hst=hst):
                            sync.wait_ge(sV.h, ffn[('s1e', ic, tt)])
                            dS2[g % 2].inc(sync.dma_start(
                                out=scr_h[tt * 128:(tt + 1) * 128, ic * 512:(ic + 1) * 512],
                                in_=hst), 16)
                        ffn[('s1s', ic, tt)] = dS2[g % 2].n
                        s1s_hist.append(dS2[g % 2].n)
                    else:
                        gst = hblk[:, 1024 * P:1024 * P + 512]
                        ust = hblk[:, 1024 * P + 512:1024 * P + 1024]
                        @B("vector")
                        def _(v, ic=ic, tt=tt, g=g, tkmm=tkmm, pg=pg, pu=pu,
                              gst=gst, ust=ust):
                            v.wait_ge(sT.h, tkmm)
                            v.wait_ge(dS2[g % 2].h, s1s_hist[g])
                            v.tensor_copy(out=gst, in_=pg[:, 0:512])
                            v.tensor_copy(out=ust, in_=pu[:, 0:512])
                            v.reduce_max(out=st[:, 41:42], in_=gst,
                                         axis=AX.X, apply_absolute_value=True)
                            sV.inc(v.reduce_max(out=st[:, 42:43], in_=ust,
                                                axis=AX.X, apply_absolute_value=True))
                        trh = sV.n

                        @B("vector")
                        def _(v, trh=trh):
                            v.wait_ge(sV.h, trh)
                            v.tensor_tensor(out=st[:, 16:17], in0=st[:, 16:17],
                                            in1=st[:, 41:42], op=ALU.max)
                            sV.inc(v.tensor_tensor(out=st[:, 17:18], in0=st[:, 17:18],
                                                   in1=st[:, 42:43], op=ALU.max))
                        ffn[('s1e', ic, tt)] = sV.n
                        s1e_hist.append(sV.n)

                        @B("sync")
                        def _(sync, ic=ic, tt=tt, g=g, gst=gst, ust=ust):
                            sync.wait_ge(sV.h, ffn[('s1e', ic, tt)])
                            dS2[g % 2].inc(sync.dma_start(
                                out=scr_g[tt * 128:(tt + 1) * 128, ic * 512:(ic + 1) * 512],
                                in_=gst), 16)
                            dS2[g % 2].inc(sync.dma_start(
                                out=scr_u[tt * 128:(tt + 1) * 128, ic * 512:(ic + 1) * 512],
                                in_=ust), 16)
                        ffn[('s1s', ic, tt)] = dS2[g % 2].n
                        s1s_hist.append(dS2[g % 2].n)

            s1_drain = (dS2[0].n, dS2[1].n)

            if layer == 0:
                @B("vector")
                def _(v):
                    sV.inc(v.tensor_scalar(out=st[:, 16:32], in0=st[:, 16:32],
                                           scalar1=1e-5, scalar2=1.0 / 127.0,
                                           op0=ALU.max, op1=ALU.mult))
                tsc0 = sV.n

                @B("vector")
                def _(v, tsc0=tsc0):
                    v.wait_ge(sV.h, tsc0)
                    v.reciprocal(out=st[:, 32:48], in_=st[:, 16:32])
                    sV.inc(v.tensor_scalar(out=st[:, 48:64], in0=st[:, 16:32],
                                           scalar1=bcol[:, 2:3], scalar2=None,
                                           op0=ALU.mult))
                tok_scales = sV.n
                quant_src, quant_sinv_is_strip = scr_h, True
            else:
                @B("vector")
                def _(v, s1_drain=s1_drain):
                    v.wait_ge(dS2[0].h, s1_drain[0])
                    v.wait_ge(dS2[1].h, s1_drain[1])
                    sV.inc(v.engine_nop())
                partition_max(16, 0)
                partition_max(17, 1)
                artok = allreduce2(sc1[0:1, 0:1], sc1[0:1, 1:2], sV.n)
                @B("vector")
                def _(v, artok=artok):
                    v.wait_ge(sG.h, artok)
                    v.tensor_tensor(out=sc1[0:1, 4:5], in0=sc1[0:1, 4:5],
                                    in1=bcol[0:1, 2:3], op=ALU.mult)
                    sV.inc(v.tensor_tensor(out=sc1[0:1, 5:6], in0=sc1[0:1, 5:6],
                                           in1=bcol[0:1, 2:3], op=ALU.mult))
                tscm = sV.n

                @B("vector")
                def _(v, tscm=tscm):
                    v.wait_ge(sV.h, tscm)
                    for col in (4, 5):
                        ins = v.tensor_scalar(out=sc1[0:1, col:col + 1],
                                              in0=sc1[0:1, col:col + 1],
                                              scalar1=1.0 / 32767.0, scalar2=1e-12,
                                              op0=ALU.mult, op1=ALU.max)
                    sV.inc(ins)
                tsca = sV.n

                @B("vector")
                def _(v, tsca=tsca):
                    v.wait_ge(sV.h, tsca)
                    v.tensor_copy(out=sc1[0:1, 2:3], in_=sc1[0:1, 5:6])
                    v.reciprocal(out=sc1[0:1, 6:7], in_=sc1[0:1, 4:5])
                    sV.inc(v.reciprocal(out=sc1[0:1, 7:8], in_=sc1[0:1, 5:6]))
                tscb = sV.n

                @B("vector")
                def _(v, tscb=tscb):
                    v.wait_ge(sV.h, tscb)
                    v.tensor_tensor(out=sc1[0:1, 6:7], in0=sc1[0:1, 6:7],
                                    in1=bcol[0:1, 2:3], op=ALU.mult)
                    sV.inc(v.tensor_tensor(out=sc1[0:1, 7:8], in0=sc1[0:1, 7:8],
                                           in1=bcol[0:1, 2:3], op=ALU.mult))
                tsc = sV.n
                gt1 = scalar_to_bcol(sc1[0:1, 6:7], 4, sV, tsc, 4)
                gt2 = scalar_to_bcol(sc1[0:1, 7:8], 5, sV, tsc, 5)
                gt3 = scalar_to_bcol(sc1[0:1, 4:5], 6, sV, tsc, 6)
                @B("vector")
                def _(v):
                    sV.inc(v.memset(st[:, 18:19], 0.0))
                p2v_hist = [0, 0]
                p2w_hist = [0, 0]
                for tt in range(TT):
                    for ic in range(NI):
                        g = tt * NI + ic
                        base = 2048 * (g % 2)
                        binp = hblk[:, base:base + 512]
                        bwork = hblk[:, base + 512:base + 1024]
                        bsig = hblk[:, base + 1024:base + 1536]
                        @B("sync")
                        def _(sync, tt=tt, ic=ic, g=g, gt3=gt3, binp=binp):
                            if tt == 0 and ic == 0:
                                sync.wait_ge(sG.h, gt3)
                            sync.wait_ge(sV.h, p2v_hist[g])
                            dD2[g % 2].inc(sync.dma_start(
                                out=binp,
                                in_=scr_g[tt * 128:(tt + 1) * 128, ic * 512:(ic + 1) * 512]), 16)
                        ffn[('p2r', tt, ic)] = dD2[g % 2].n

                        @B("vector")
                        def _(v, tt=tt, ic=ic, g=g, binp=binp, bwork=bwork):
                            v.wait_ge(dD2[g % 2].h, ffn[('p2r', tt, ic)])
                            v.wait_ge(dS2[g % 2].h, p2w_hist[g])
                            v.tensor_scalar(out=bwork, in0=binp,
                                            scalar1=bcol[:, 4:5], scalar2=MAGIC,
                                            op0=ALU.mult, op1=ALU.add)
                            sV.inc(v.tensor_scalar(out=bwork, in0=bwork,
                                                   scalar1=MAGIC, scalar2=None,
                                                   op0=ALU.subtract))
                        tgi = sV.n

                        @B("scalar")
                        def _(s, tgi=tgi, bwork=bwork, bsig=bsig):
                            s.wait_ge(sV.h, tgi)
                            sS.inc(s.activation(out=bsig, in_=bwork,
                                                func=AF.Sigmoid, scale=bcol[:, 6:7]))
                        tsig = sS.n

                        @B("vector")
                        def _(v, tsig=tsig, bwork=bwork, bsig=bsig):
                            v.wait_ge(sS.h, tsig)
                            v.tensor_scalar(out=bsig, in0=bsig,
                                            scalar1=65536.0, scalar2=MAGIC,
                                            op0=ALU.mult, op1=ALU.add)
                            v.tensor_scalar(out=bsig, in0=bsig,
                                            scalar1=MAGIC, scalar2=65535.0,
                                            op0=ALU.subtract, op1=ALU.min)
                            v.tensor_scalar(out=bwork, in0=bwork,
                                            scalar1=bcol[:, 6:7], scalar2=1.0 / 65536.0,
                                            op0=ALU.mult, op1=ALU.mult)
                            v.tensor_tensor(out=bwork, in0=bwork, in1=bsig, op=ALU.mult)
                            sV.inc(v.reduce_max(out=st[:, 41:42], in_=bwork,
                                                axis=AX.X, apply_absolute_value=True))
                        trh2 = sV.n

                        @B("vector")
                        def _(v, trh2=trh2):
                            v.wait_ge(sV.h, trh2)
                            sV.inc(v.tensor_tensor(out=st[:, 18:19], in0=st[:, 18:19],
                                                   in1=st[:, 41:42], op=ALU.max))
                        ffn[('p2v', tt, ic)] = sV.n
                        p2v_hist.append(sV.n)

                        @B("sync")
                        def _(sync, tt=tt, ic=ic, g=g, bwork=bwork):
                            sync.wait_ge(sV.h, ffn[('p2v', tt, ic)])
                            dS2[g % 2].inc(sync.dma_start(
                                out=scr_g2[tt * 128:(tt + 1) * 128, ic * 512:(ic + 1) * 512],
                                in_=bwork), 16)
                        ffn[('p2w', tt, ic)] = dS2[g % 2].n
                        p2w_hist.append(dS2[g % 2].n)
                p2_drain = (dS2[0].n, dS2[1].n)
                @B("vector")
                def _(v, p2_drain=p2_drain):
                    v.wait_ge(dS2[0].h, p2_drain[0])
                    v.wait_ge(dS2[1].h, p2_drain[1])
                    sV.inc(v.engine_nop())
                partition_max(18, 0)
                artok = allreduce2(sc1[0:1, 0:1], sc1[0:1, 0:1], sV.n)
                @B("vector")
                def _(v, artok=artok):
                    v.wait_ge(sG.h, artok)
                    sV.inc(v.tensor_scalar(out=sc1[0:1, 4:5], in0=sc1[0:1, 4:5],
                                           scalar1=1.0 / 32767.0, scalar2=1e-12,
                                           op0=ALU.mult, op1=ALU.max))
                tsc2a = sV.n

                @B("vector")
                def _(v, tsc2a=tsc2a):
                    v.wait_ge(sV.h, tsc2a)
                    sV.inc(v.reciprocal(out=sc1[0:1, 6:7], in_=sc1[0:1, 4:5]))
                tsc2 = sV.n
                gt4 = scalar_to_bcol(sc1[0:1, 6:7], 6, sV, tsc2, 4)
                @B("vector")
                def _(v):
                    sV.inc(v.tensor_tensor(out=sc1[0:1, 7:8], in0=sc1[0:1, 4:5],
                                           in1=sc1[0:1, 2:3], op=ALU.mult))
                gt5 = scalar_to_bcol(sc1[0:1, 7:8], 7, sV, sV.n, 5)
                @B("vector")
                def _(v):
                    sV.inc(v.memset(st[:, 19:20], 0.0))
                p3v_hist = [0, 0]
                p3w_hist = [0, 0]
                for tt in range(TT):
                    for ic in range(NI):
                        g = tt * NI + ic
                        base = 2048 * (g % 2)
                        bg2 = hblk[:, base:base + 512]
                        bui = hblk[:, base + 512:base + 1024]
                        bop = hblk[:, base + 1024:base + 1536]
                        bsc = hblk[:, base + 1536:base + 2048]
                        @B("sync")
                        def _(sync, tt=tt, ic=ic, g=g, gt5=gt5, bg2=bg2, bui=bui):
                            if tt == 0 and ic == 0:
                                sync.wait_ge(sG.h, gt5)
                            sync.wait_ge(sV.h, p3v_hist[g])
                            dD2[g % 2].inc(sync.dma_start(
                                out=bg2,
                                in_=scr_g2[tt * 128:(tt + 1) * 128, ic * 512:(ic + 1) * 512]), 16)
                            dD2[g % 2].inc(sync.dma_start(
                                out=bui,
                                in_=scr_u[tt * 128:(tt + 1) * 128, ic * 512:(ic + 1) * 512]), 16)
                        ffn[('p3r', tt, ic)] = dD2[g % 2].n

                        @B("vector")
                        def _(v, tt=tt, ic=ic, g=g, bg2=bg2, bui=bui, bop=bop, bsc=bsc):
                            v.wait_ge(dD2[g % 2].h, ffn[('p3r', tt, ic)])
                            v.wait_ge(dS2[g % 2].h, p3w_hist[g])
                            v.tensor_scalar(out=bop, in0=bg2,
                                            scalar1=bcol[:, 6:7], scalar2=MAGIC,
                                            op0=ALU.mult, op1=ALU.add)
                            v.tensor_scalar(out=bop, in0=bop,
                                            scalar1=MAGIC, scalar2=None, op0=ALU.subtract)
                            v.tensor_scalar(out=bsc, in0=bui,
                                            scalar1=bcol[:, 5:6], scalar2=MAGIC,
                                            op0=ALU.mult, op1=ALU.add)
                            v.tensor_scalar(out=bsc, in0=bsc,
                                            scalar1=MAGIC, scalar2=None, op0=ALU.subtract)
                            v.scalar_tensor_tensor(out=bop, in0=bop,
                                                   scalar=bcol[:, 7:8],
                                                   in1=bsc,
                                                   op0=ALU.mult, op1=ALU.mult)
                            sV.inc(v.reduce_max(out=st[:, 41:42], in_=bop,
                                                axis=AX.X, apply_absolute_value=True))
                        trh3 = sV.n

                        @B("vector")
                        def _(v, trh3=trh3):
                            v.wait_ge(sV.h, trh3)
                            sV.inc(v.tensor_tensor(out=st[:, 19:20], in0=st[:, 19:20],
                                                   in1=st[:, 41:42], op=ALU.max))
                        ffn[('p3v', tt, ic)] = sV.n
                        p3v_hist.append(sV.n)

                        @B("sync")
                        def _(sync, tt=tt, ic=ic, g=g, bop=bop):
                            sync.wait_ge(sV.h, ffn[('p3v', tt, ic)])
                            dS2[g % 2].inc(sync.dma_start(
                                out=scr_h[tt * 128:(tt + 1) * 128, ic * 512:(ic + 1) * 512],
                                in_=bop), 16)
                        ffn[('p3w', tt, ic)] = dS2[g % 2].n
                        p3w_hist.append(dS2[g % 2].n)
                p3_drain = (dS2[0].n, dS2[1].n)
                @B("vector")
                def _(v, p3_drain=p3_drain):
                    v.wait_ge(dS2[0].h, p3_drain[0])
                    v.wait_ge(dS2[1].h, p3_drain[1])
                    sV.inc(v.engine_nop())
                partition_max(19, 0)
                artok = allreduce2(sc1[0:1, 0:1], sc1[0:1, 0:1], sV.n)
                @B("vector")
                def _(v, artok=artok):
                    v.wait_ge(sG.h, artok)
                    sV.inc(v.tensor_scalar(out=sc1[0:1, 4:5], in0=sc1[0:1, 4:5],
                                           scalar1=1.0 / 32767.0, scalar2=1e-12,
                                           op0=ALU.mult, op1=ALU.max))
                tsoa = sV.n

                @B("vector")
                def _(v, tsoa=tsoa):
                    v.wait_ge(sV.h, tsoa)
                    sV.inc(v.reciprocal(out=sc1[0:1, 6:7], in_=sc1[0:1, 4:5]))
                tso = sV.n
                gt6 = scalar_to_bcol(sc1[0:1, 6:7], 6, sV, tso, 6)
                gt7 = scalar_to_bcol(sc1[0:1, 4:5], 7, sV, tso, 7)
                tok_scales = sG.n
                quant_src = scr_h
                s2_src_tok = p3_drain

            NI128 = I // 128
            for tb in range(4):
                for tq in range(4):
                    tt = tb * 4 + tq
                    @B("sync")
                    def _(sync, tt=tt, tb=tb, tq=tq):
                        drain = s1_drain if layer == 0 else s2_src_tok
                        sync.wait_ge(dS2[0].h, drain[0])
                        sync.wait_ge(dS2[1].h, drain[1])
                        sync.wait_ge(sV.h, ffn.get(('s2q', tt - 1), 0))
                        dW.inc(sync.dma_start(out=hblk[:, 0:I],
                                              in_=quant_src[tt * 128:(tt + 1) * 128, 0:I]), 16)
                    ffn[('s2r', tt)] = dW.n

                    @B("vector")
                    def _(v, tt=tt):
                        v.wait_ge(dW.h, ffn[('s2r', tt)])
                        if tt == 0:
                            v.wait_ge(sV.h if layer == 0 else sG.h, tok_scales)
                            v.wait_ge(sT.h, ffn.get(('s2mmlast', -1), 0))
                        sinv = st[:, 32 + tt:33 + tt] if layer == 0 else bcol[:, 6:7]
                        v.tensor_scalar(out=hblk[:, 0:I], in0=hblk[:, 0:I],
                                        scalar1=sinv, scalar2=MAGIC,
                                        op0=ALU.mult, op1=ALU.add)
                        sV.inc(v.tensor_scalar(out=qtile[:, 0:I], in0=hblk[:, 0:I],
                                               scalar1=MAGIC, scalar2=None,
                                               op0=ALU.subtract))
                    ffn[('s2q', tt)] = sV.n

                    for b0 in range(0, NI128, 8):
                        @B("tensor")
                        def _(t, tt=tt, tq=tq, b0=b0):
                            t.wait_ge(sV.h, ffn[('s2q', tt)])
                            t.wait_ge(sV.h, pt_free[0])
                            for i in range(8):
                                ins = t.transpose(pt[:, i * 128:(i + 1) * 128],
                                                  qtile[:, (b0 + i) * 128:(b0 + i + 1) * 128],
                                                  ident[:])
                            sT.inc(ins)
                        tk = sT.n

                        @B("vector")
                        def _(v, tq=tq, b0=b0, tk=tk):
                            v.wait_ge(sT.h, tk)
                            for i in range(8):
                                ins = v.tensor_copy(
                                    out=hqfm[:, b0 + i, tq * 128:(tq + 1) * 128],
                                    in_=pt[:, i * 128:(i + 1) * 128])
                            sV.inc(ins)
                        pt_free[0] = sV.n
                    ffn[('s2hq', tt)] = sV.n

                for cb in range(2):
                    for icc in range(NI128):
                        @B("sync")
                        def _(sync, tb=tb, cb=cb, icc=icc):
                            if icc >= 2:
                                sync.wait_ge(sT.h, ffn[('s2mm', tb, cb, icc - 2)])
                            else:
                                sync.wait_ge(sT.h, ffn.get(('s2mmlast', -1), 0))
                            src = wd_scr[icc * 128:(icc + 1) * 128,
                                         cb * 512:(cb + 1) * 512]
                            dD2[icc % 2].inc(sync.dma_start(out=wdd[:, icc % 2, 0:512],
                                                            in_=src), 16)
                        ffn[('s2wd', tb, cb, icc)] = dD2[icc % 2].n

                        @B("tensor")
                        def _(t, tb=tb, cb=cb, icc=icc):
                            t.wait_ge(dD2[icc % 2].h, ffn[('s2wd', tb, cb, icc)])
                            if icc == 0:
                                t.wait_ge(sV.h, ffn[('s2hq', tb * 4 + 3)])
                                t.wait_ge(sV.h, ffn.get(('s2ev', tb, cb - 1, 3), oproj_evac[0]))
                            for tq in range(4):
                                ins = t.matmul(ps_av[tq][:, 0:512],
                                               hqfm[:, icc, tq * 128:(tq + 1) * 128],
                                               wdd[:, icc % 2, 0:512],
                                               start=(icc == 0), stop=(icc == NI128 - 1))
                            sT.inc(ins)
                        ffn[('s2mm', tb, cb, icc)] = sT.n
                    ffn[('s2mmlast', -1)] = sT.n

                    for tq in range(4):
                        @B("vector")
                        def _(v, tb=tb, cb=cb, tq=tq):
                            v.wait_ge(sT.h, ffn[('s2mm', tb, cb, NI128 - 1)])
                            tt = tb * 4 + tq
                            bt = st[:, 48 + tt:49 + tt] if layer == 0 else bcol[:, 7:8]
                            sV.inc(v.scalar_tensor_tensor(
                                out=xm[:, tt, cb * 512:(cb + 1) * 512],
                                in0=ps_av[tq][:, 0:512], scalar=bt,
                                in1=xm[:, tt, cb * 512:(cb + 1) * 512],
                                op0=ALU.mult, op1=ALU.add))
                        ffn[('s2ev', tb, cb, tq)] = sV.n
                    oproj_evac[0] = sV.n

        ln_retranspose(first=True)
        attention(0)
        if stop_after != "a0":
            ln_retranspose(mode=ln1_mode)
        if stop_after not in ("x1", "a0"):
            ffn_layer(0)
            ln_retranspose()
        if stop_after not in ("x1", "a0", "x2"):
            if dbg:
                @B("gpsimd")
                def _(g, tok=sV.n):
                    g.wait_ge(sV.h, tok)
                    dG.inc(g.dma_start(out=dbg_x2,
                                       in_=xm.rearrange("p t c -> p (t c)")), 16)
                    g.wait_ge(dG.h, dG.n)
                    sG.inc(g.nop())
            attention(1)
            ln_retranspose()
            if dbg:
                @B("gpsimd")
                def _(g, tok=sV.n):
                    g.wait_ge(sV.h, tok)
                    dG.inc(g.dma_start(out=dbg_x3,
                                       in_=xm.rearrange("p t c -> p (t c)")), 16)
                    g.wait_ge(dG.h, dG.n)
                    sG.inc(g.nop())
        if stop_after not in ("x1", "a0", "x2", "x3"):
            ffn_layer(1)
            ln_retranspose()

        @B("sync")
        def _(sync):
            sync.wait_ge(sV.h, sV.n)
            xt = out.rearrange("(n p) c -> p n c", p=128)
            for i in range(4):
                dW.inc(sync.dma_start(out=xt[:, i * 4:(i + 1) * 4, :],
                                      in_=xm[:, i * 4:(i + 1) * 4, :]), 16)
            sync.wait_ge(dW.h, dW.n)
        if dbg:
            @B("gpsimd")
            def _(g):
                g.wait_ge(sV.h, sV.n)
                g.wait_ge(sT.h, sT.n)
                g.wait_ge(sS.h, sS.n)
                if ln1_mode == "stats":
                    xf_pair = (dbg_xf[:, 0:4096], hblk[:, 0:4096])
                else:
                    xf_pair = (dbg_xf, xfm.rearrange("p a t -> p (a t)"))
                for dst, src in ((dbg_qt, big2[:, 0:3072]),
                                 (dbg_qf, qf), (dbg_kf, kf), (dbg_vf, vf),
                                 (dbg_pp, pp),
                                 (dbg_v1, v1.rearrange("p t o -> p (t o)")),
                                 (dbg_yp, ypair.rearrange("p a t -> p (a t)")),
                                 (dbg_st, st[:, :]), (dbg_ng, negC[:, :]),
                                 (dbg_bc, bcol[:, :]),
                                 xf_pair):
                    dG.inc(g.dma_start(out=dst, in_=src), 16)
                g.wait_ge(dG.h, dG.n)
    return nc


KEYMAP = {
    "x": "L0_attn_x", "wq": "L0_attn_w_q", "wk": "L0_attn_w_k",
    "wv": "L0_attn_w_v", "wo": "L0_attn_w_o", "g0": "L0_ffn_w_gate",
    "u0": "L0_ffn_w_up", "d0": "L0_ffn_w_down", "wa": "L1_attn_w_attn",
    "wp": "L1_attn_w_proj", "g1": "L1_ffn_w_gate", "u1": "L1_ffn_w_up",
    "d1": "L1_ffn_w_down",
}


def make_in_maps(inputs):
    x = np.ascontiguousarray(np.asarray(inputs["L0_attn_x"], np.float32))
    base = {short: np.ascontiguousarray(np.asarray(inputs[long], np.float32))
            for short, long in KEYMAP.items() if short != "x"}
    return [dict(base, x=x[b]) for b in range(8)]


def kernel(**inputs):
    nc = build(no_cc=True)
    res = run_bass_kernel_spmd(nc, make_in_maps(inputs), core_ids=list(range(8)))
    return np.stack([np.asarray(r["out"]) for r in res.results], axis=0)
